# version 02 (final)
# baseline (speedup 1.0000x reference)
import os
import numpy as np
import ml_dtypes
import concourse.bass as bass
import concourse.mybir as mybir
from concourse.bass_utils import run_bass_kernel_spmd
from concourse.alu_op_type import AluOpType as A
from contextlib import ExitStack

F32 = mybir.dt.float32
BF16 = mybir.dt.bfloat16
U8 = mybir.dt.uint8
AF = mybir.ActivationFunctionType
BF = ml_dtypes.bfloat16
f = np.float32

NT = 8192
D = 1024
T = 2048
NSEQ = 4
NH = 16
DH = 64
R = 32
NE = 8
HID = 1536
F1 = 4096
ISCALE = f(1.0 / np.sqrt(32.0))

ENGINES = ("sync", "gpsimd", "tensor", "vector", "scalar")


class B:

    DMA_POOL = {"sync": ["ls%d" % i for i in range(6)] + ["ss%d" % i for i in range(6)],
                "gpsimd": ["lg%d" % i for i in range(6)] + ["sg%d" % i for i in range(2)]}

    def __init__(self, nc, st):
        self.nc = nc
        self.sched = []
        self.sem = {}
        names = ["t", "v", "s"]
        for pool in self.DMA_POOL.values():
            names += pool
        for name in names:
            self.sem[name] = [st.enter_context(nc.semaphore("sem_" + name)), 0]
        self.cur = {("sync", "l"): 0, ("sync", "s"): 0, ("gpsimd", "l"): 0, ("gpsimd", "s"): 0}

    def val(self, name):
        return self.sem[name][1]

    def op(self, eng, fn, waits=(), inc=None, inc_k=1):
        tgt = None
        if inc is not None:
            self.sem[inc][1] += inc_k
            tgt = self.sem[inc][1]
        clean = []
        def _add(w):
            if w is None:
                return
            if isinstance(w, list):
                for x in w:
                    _add(x)
                return
            s, v = w
            if v and v > 0:
                clean.append((s, v))
        for w in waits:
            _add(w)
        waits = tuple(clean)
        semd = self.sem

        def run(E, fn=fn, waits=waits, inc=inc, inc_k=inc_k):
            for s_, v_ in waits:
                E.wait_ge(semd[s_][0], v_)
            r = fn(E)
            if inc is not None:
                r.then_inc(semd[inc][0], inc_k)

        self.sched.append((eng, run))
        return tgt

    def wait(self, eng, *waits):
        clean = []
        def _add(w):
            if w is None:
                return
            if isinstance(w, list):
                for x in w:
                    _add(x)
                return
            s, v = w
            if v and v > 0:
                clean.append((s, v))
        for w in waits:
            _add(w)
        if not clean:
            return
        semd = self.sem
        self.sched.append((eng, lambda E, w=tuple(clean): [E.wait_ge(semd[s][0], v) for s, v in w]))

    def _dma(self, q, kind, fn, waits):
        nsem = 6 if kind == "l" else (6 if q == "sync" else 2)
        base = {"l": "ls" if q == "sync" else "lg", "s": "ss" if q == "sync" else "sg"}[kind]
        i = self.cur[(q, kind)]
        self.cur[(q, kind)] = (i + 1) % nsem
        sem = f"{base}{i}"
        prev = self.val(sem)
        tgt = self.op(q, fn, waits=tuple(waits) + ((sem, prev),), inc=sem, inc_k=16)
        return (sem, tgt)

    def load(self, q, dst, src, waits=()):
        return self._dma(q, "l", lambda E, d=dst, s=src: E.dma_start(d, s), waits)

    def loadT(self, dst, src, waits=()):
        return self._dma("sync", "l", lambda E, d=dst, s=src: E.dma_start(d, s, transpose=True), waits)

    def store(self, q, dst, src, waits=()):
        return self._dma(q, "s", lambda E, d=dst, s=src: E.dma_start(d, s), waits)

    def drain(self, eng=None):
        for q, engq in (("sync", "sync"), ("gpsimd", "gpsimd")):
            w = [(sname, self.val(sname)) for sname in self.DMA_POOL[q]]
            self.wait(engq, *w)

    def flush(self):
        sched = self.sched
        self.sched = []
        if not sched:
            return
        with self.nc.Block() as blk:
            for eng in ENGINES:
                ops = [fnn for e, fnn in sched if e == eng]
                if ops:
                    getattr(blk, eng)(lambda EE, ops=ops: [o(EE) for o in ops])


class Kern:
    def __init__(self, dbg=(), upto=99):
        self.dbg = set(dbg)
        self.upto = upto
        self.nc = bass.Bass("TRN2", target_bir_lowering=False)
        self.st = ExitStack()
        self.dram = {}

    def dt(self, name, shape, dtype, kind="Internal"):
        if name in self.dbg and kind == "Internal":
            kind = "ExternalOutput"
        ap = self.nc.dram_tensor(name, shape, dtype, kind=kind).ap()
        self.dram[name] = ap
        return ap

    def sb(self, name, shape, dtype):
        return self.st.enter_context(self.nc.sbuf_tensor(name, shape, dtype))

    def ps(self, name, shape, dtype):
        return self.st.enter_context(self.nc.psum_tensor(name, shape, dtype))

    def build(self):
        nc, dt = self.nc, self.dt
        dt("x", [NT, D], F32, "ExternalInput")
        for n in ("wq", "wk", "wv", "wo", "wq1", "wk1", "wv1", "wp"):
            dt(n, [D, D], F32, "ExternalInput")
        dt("bq8", [128, 8], F32, "ExternalInput")
        dt("bk8", [128, 8], F32, "ExternalInput")
        dt("BV", [128, D], F32, "ExternalInput")
        dt("BO", [128, D], F32, "ExternalInput")
        dt("BP", [128, D], F32, "ExternalInput")
        dt("lsrq", [D, R], F32, "ExternalInput")
        dt("lsrk", [D, R], F32, "ExternalInput")
        dt("wr", [D, 8], F32, "ExternalInput")
        dt("w1", [NE * D, HID], F32, "ExternalInput")
        dt("w3", [NE * D, HID], F32, "ExternalInput")
        dt("w2", [NE * HID, D], F32, "ExternalInput")
        dt("cw", [128, 24], F32, "ExternalInput")
        dt("cb", [128, 8], F32, "ExternalInput")
        dt("k0", [2 * D, F1], U8, "ExternalInput")
        dt("k1", [F1, F1], U8, "ExternalInput")
        dt("k2", [F1, D], U8, "ExternalInput")
        dt("s0c", [128, 1], F32, "ExternalInput")
        dt("s1c", [128, 1], F32, "ExternalInput")
        dt("s2c", [128, 1], F32, "ExternalInput")
        dt("ident", [128, 128], BF16, "ExternalInput")
        dt("triu", [128, 128], BF16, "ExternalInput")
        dt("y", [NT, D], F32, "ExternalOutput")
        for n in ("wqb", "wkb", "wvb", "wob", "wq1b", "wk1b", "wv1b", "wpb"):
            dt(n, [D, D], BF16)
        dt("lsrqb", [D, R], BF16)
        dt("lsrkb", [D, R], BF16)
        dt("wrb", [D, 8], BF16)
        dt("w1b", [NE * D, HID], BF16)
        dt("w3b", [NE * D, HID], BF16)
        dt("w2b", [NE * HID, D], BF16)
        dt("w0p", [D, F1], BF16)
        dt("k1p", [32 * F1, 128], BF16)
        dt("k2p", [F1, D], BF16)
        dt("xnT", [D, NT], BF16)
        dt("qt0", [D, NT], BF16)
        dt("kt0", [D, NT], BF16)
        dt("v0", [NT, D], BF16)
        dt("xpart", [NT, D], F32)
        dt("x1", [NT, D], F32)
        dt("xnT1", [D, NT], BF16)
        dt("qt1", [D, NT], BF16)
        dt("k1tok", [NT, D], BF16)
        dt("v1tok", [NT, D], BF16)
        dt("xpart1", [NT, D], F32)
        dt("r0T", [F1, NT], BF16)
        dt("r1T", [F1, NT], BF16)
        if "combdbg" in self.dbg:
            self.dt("combdbg", [128, 64 * 8], F32, "ExternalOutput")

        g = {}
        g["ident"] = self.sb("g_ident", [128, 128], BF16)
        g["triu"] = self.sb("g_triu", [128, 128], BF16)
        g["bq8"] = self.sb("g_bq8", [128, 8], F32)
        g["bk8"] = self.sb("g_bk8", [128, 8], F32)
        g["BV"] = self.sb("g_BV", [128, D], F32)
        g["BO"] = self.sb("g_BO", [128, D], F32)
        g["BP"] = self.sb("g_BP", [128, D], F32)
        g["cw"] = self.sb("g_cw", [128, 24], F32)
        g["cb"] = self.sb("g_cb", [128, 8], F32)
        g["s0"] = self.sb("g_s0", [128, 1], F32)
        g["s1"] = self.sb("g_s1", [128, 1], F32)
        g["s2"] = self.sb("g_s2", [128, 1], F32)
        g["comb"] = self.sb("g_comb", [128, 64 * 8], F32)
        g["ones64"] = self.sb("g_ones64", [1, 64], BF16)
        self.g = g

        b = B(nc, self.st)
        self.b = b

        self.phase_prep()
        if self.upto >= 1:
            self.phase_ln(0)
        if self.upto >= 2:
            self.phase_proj0()
        if self.upto >= 3:
            self.phase_attn0()
        if self.upto >= 4:
            self.phase_moe()
        if self.upto >= 5:
            self.phase_ln(1)
        if self.upto >= 6:
            self.phase_proj1()
        if self.upto >= 7:
            self.phase_attn1()
        if self.upto >= 8:
            self.phase_ffn1a()
        if self.upto >= 9:
            self.phase_ffn1b()
        if self.upto >= 10:
            self.phase_ffn1c()

        if "combdbg" in self.dbg:
            b.store("sync", self.dram["combdbg"], g["comb"][:])
        b.drain()
        b.flush()
        self.st.close()
        return nc

    def phase_prep(self):
        b, d, g, nc = self.b, self.dram, self.g, self.nc
        for name, dst in (("ident", g["ident"]), ("triu", g["triu"]), ("bq8", g["bq8"]),
                          ("bk8", g["bk8"]), ("BV", g["BV"]), ("BO", g["BO"]), ("BP", g["BP"]),
                          ("cw", g["cw"]), ("cb", g["cb"]), ("s0c", g["s0"]),
                          ("s1c", g["s1"]), ("s2c", g["s2"])):
            b.load("sync", dst[:], d[name])
        wsmall = [(sname, b.val(sname)) for sname in b.DMA_POOL["sync"][:6]]
        b.op("vector", lambda E: E.tensor_scalar(g["s0"][:], g["s0"][:], 4.0, None, A.mult),
             waits=wsmall, inc="v")
        b.op("vector", lambda E: E.tensor_scalar(g["s1"][:], g["s1"][:], 2.0, None, A.mult), inc="v")
        b.op("vector", lambda E: E.memset(g["ones64"][:], 1.0), inc="v")

        cast_jobs = []
        for n in ("wq", "wk", "wv", "wo", "wq1", "wk1", "wv1", "wp"):
            cast_jobs.append((d[n], d[n + "b"], D, D))
        cast_jobs.append((d["lsrq"], d["lsrqb"], D, R))
        cast_jobs.append((d["lsrk"], d["lsrkb"], D, R))
        cast_jobs.append((d["wr"], d["wrb"], D, 8))
        cast_jobs.append((d["w1"], d["w1b"], NE * D, HID))
        cast_jobs.append((d["w3"], d["w3b"], NE * D, HID))
        cast_jobs.append((d["w2"], d["w2b"], NE * HID, D))

        st0 = ExitStack()
        cbuf = [st0.enter_context(nc.sbuf_tensor(f"p0_cb{i}", [128, HID], BF16)) for i in range(4)]
        st_at = [None, None, None, None]
        bi = 0
        for (src, dst, rows, cols) in cast_jobs:
            for r0 in range(0, rows, 128):
                buf = cbuf[bi % 4]
                ld = b.load("gpsimd", buf[:, 0:cols], src[r0:r0 + 128, :],
                            waits=[st_at[bi % 4]])
                st_at[bi % 4] = b.store("sync", dst[r0:r0 + 128, :], buf[:, 0:cols],
                                        waits=[ld])
                bi += 1

        ka = [st0.enter_context(nc.sbuf_tensor(f"p0_ka{i}", [128, F1], BF16)) for i in range(2)]
        kb_ = [st0.enter_context(nc.sbuf_tensor(f"p0_kb{i}", [128, F1], BF16)) for i in range(2)]
        ko = [st0.enter_context(nc.sbuf_tensor(f"p0_ko{i}", [128, F1], BF16)) for i in range(2)]
        ko_free = [None, None]
        ab_free = [0, 0]
        for i in range(8):
            ii = i % 2
            la = b.load("gpsimd", ka[ii][:], d["k0"][i * 128:(i + 1) * 128, :],
                        waits=[("v", ab_free[ii])])
            lb = b.load("gpsimd", kb_[ii][:], d["k0"][D + i * 128:D + (i + 1) * 128, :])
            ab_free[ii] = b.op("vector",
                               lambda E, ii=ii: E.tensor_tensor(ko[ii][:], ka[ii][:], kb_[ii][:], A.subtract),
                               waits=[lb, ko_free[ii]], inc="v")
            ko_free[ii] = b.store("sync", d["w0p"][i * 128:(i + 1) * 128, :], ko[ii][:],
                                  waits=[("v", ab_free[ii])])
        k1v = d["k1p"].rearrange("(m j) c -> j m c", m=32)
        for jt in range(32):
            ii = jt % 2
            la = b.load("gpsimd", ka[ii][:], d["k1"][jt * 128:(jt + 1) * 128, :],
                        waits=[("v", ab_free[ii])])
            ab_free[ii] = b.op("vector",
                               lambda E, ii=ii: E.tensor_scalar(ko[ii][:], ka[ii][:], 2.0, -1.0, A.mult, A.add),
                               waits=[la, ko_free[ii]], inc="v")
            ko_free[ii] = b.store("sync", k1v[jt * 128:(jt + 1) * 128, :, :],
                                  ko[ii][:].rearrange("p (m c) -> p m c", m=32),
                                  waits=[("v", ab_free[ii])])
        for jt in range(32):
            ii = jt % 2
            la = b.load("gpsimd", ka[ii][:, 0:D], d["k2"][jt * 128:(jt + 1) * 128, :],
                        waits=[("v", ab_free[ii])])
            ab_free[ii] = b.op("vector",
                               lambda E, ii=ii: E.tensor_scalar(ko[ii][:, 0:D], ka[ii][:, 0:D], 2.0, -1.0, A.mult, A.add),
                               waits=[la, ko_free[ii]], inc="v")
            ko_free[ii] = b.store("sync", d["k2p"][jt * 128:(jt + 1) * 128, :], ko[ii][:, 0:D],
                                  waits=[("v", ab_free[ii])])
        b.drain()
        b.flush()
        st0.close()

    def phase_ln(self, layer):
        b, d, g, nc = self.b, self.dram, self.g, self.nc
        src = d["x"] if layer == 0 else d["x1"]
        dst = d["xnT"] if layer == 0 else d["xnT1"]
        gate = (layer == 0)
        with ExitStack() as st:
            sbn = lambda n, sh, ty: st.enter_context(nc.sbuf_tensor(f"ln{layer}_{n}", sh, ty))
            psn = lambda n, sh, ty: st.enter_context(nc.psum_tensor(f"ln{layer}_{n}", sh, ty))
            X = [sbn("X0", [128, D], F32), sbn("X1", [128, D], F32)]
            SQ = sbn("SQ", [128, D], F32)
            ST = sbn("ST", [128, 8], F32)
            XN = [sbn("XN0", [128, D], BF16), sbn("XN1", [128, D], BF16)]
            XT = [sbn("XT0", [128, D], BF16), sbn("XT1", [128, D], BF16)]
            ES = sbn("ES", [128, 8], F32)
            IM = sbn("IM", [128, 8], F32)
            L2 = sbn("L2", [128, 8], F32)
            G2 = sbn("G2", [128, 8], F32)
            PGS = sbn("PGS", [128, 8], F32)
            EN = sbn("EN", [128, 8], F32)
            SS = sbn("SS", [128, 4], F32)
            WRB = sbn("WRB", [128, 64], BF16)
            pTR = [psn("pTR0", [128, 1024], BF16), psn("pTR1", [128, 1024], BF16)]
            pG = psn("pG", [128, 512], F32)

            wrb_ld = 0
            if gate:
                wrb_ld = b.load("sync", WRB[:].rearrange("p (k c) -> p k c", c=8),
                                d["wrb"].rearrange("(k p) c -> p k c", p=128))
            x_free = [0, 0]
            xn_free = [0, 0]
            xt_free = [None, None]
            tr_free = [0, 0]
            pg_free = 0
            pg_free_s = 0
            pgs_es = 0
            dstv = dst.rearrange("(a p) t -> p a t", p=128)
            for tt in range(64):
                ii = tt % 2
                ld = b.load("gpsimd", X[ii][:], src[tt * 128:(tt + 1) * 128, :],
                            waits=[("v", x_free[ii])])
                b.op("vector", lambda E, ii=ii: E.tensor_tensor(SQ[:], X[ii][:], X[ii][:], A.mult),
                     waits=[ld], inc="v")
                b.op("vector", lambda E, ii=ii: E.reduce_sum(ST[:, 0:1], X[ii][:], axis=mybir.AxisListType.X), inc="v")
                b.op("vector", lambda E: E.reduce_sum(ST[:, 1:2], SQ[:], axis=mybir.AxisListType.X), inc="v")
                b.op("vector", lambda E: E.tensor_scalar(ST[:, 2:3], ST[:, 0:1], 1.0 / D, None, A.mult), inc="v")
                vc1 = b.op("vector", lambda E: E.tensor_scalar(ST[:, 3:4], ST[:, 1:2], 1.0 / D, None, A.mult), inc="v")
                vm4 = b.op("vector", lambda E: E.tensor_tensor(ST[:, 4:5], ST[:, 2:3], ST[:, 2:3], A.mult),
                           waits=[("v", vc1)], inc="v")
                vstat = b.op("vector", lambda E: E.tensor_tensor(ST[:, 5:6], ST[:, 3:4], ST[:, 4:5], A.subtract),
                             waits=[("v", vm4)], inc="v")
                sln = b.op("scalar", lambda E: E.activation(ST[:, 6:7], ST[:, 5:6], AF.Ln, bias=f(1e-5)),
                           waits=[("v", vstat)], inc="s")
                srstd = b.op("scalar", lambda E: E.activation(ST[:, 7:8], ST[:, 6:7], AF.Exp, scale=f(-0.5)),
                             waits=[("s", sln)], inc="s")
                vxn = b.op("vector",
                           lambda E, ii=ii: E.tensor_scalar(XN[ii][:], X[ii][:], ST[:, 2:3], ST[:, 7:8], A.subtract, A.mult),
                           waits=[("s", srstd), ("t", xn_free[ii])], inc="v")
                x_free[ii] = vxn
                vcps = 0
                for ft in range(8):
                    fb = ft % 2
                    tr = b.op("tensor",
                              lambda E, ii=ii, ft=ft, fb=fb: E.transpose(pTR[fb][:, 0:128], XN[ii][:, ft * 128:(ft + 1) * 128], g["ident"][:]),
                              waits=[("v", vxn if ft < 2 else 0), ("v", tr_free[fb]),
                                     (xt_free[ii] if ft == 0 else None)], inc="t")
                    if ft >= 6:
                        xn_free[ii] = tr
                    tr_free[fb] = b.op("vector",
                                       lambda E, ii=ii, ft=ft, fb=fb: E.tensor_copy(XT[ii][:, ft * 128:(ft + 1) * 128], pTR[fb][:, 0:128]),
                                       waits=[("t", tr)], inc="v")
                    vcps = tr_free[fb]
                xt_free[ii] = b.store("sync", dstv[:, :, tt * 128:(tt + 1) * 128],
                                      XT[ii][:].rearrange("p (a c) -> p a c", a=8),
                                      waits=[("v", vcps)])
                if gate:
                    for k in range(8):
                        b.op("tensor",
                             lambda E, ii=ii, k=k: E.matmul(pG[:, 0:8], XT[ii][:, k * 128:(k + 1) * 128], WRB[:, k * 8:(k + 1) * 8], start=(k == 0), stop=(k == 7)),
                             waits=[("v", vcps if k == 0 else 0),
                                    (wrb_ld if tt == 0 and k == 0 else None),
                                    ("v", pg_free if k == 0 else 0),
                                    ("s", pg_free_s if k == 0 else 0)],
                             inc="t" if k == 7 else None)
                    tg_ = b.val("t")
                    pg_free = b.op("vector", lambda E: E.tensor_copy(PGS[:], pG[:, 0:8]),
                                   waits=[("t", tg_), ("s", pgs_es)], inc="v")
                    p1 = b.op("vector", lambda E: E.reduce_max(SS[:, 0:1], PGS[:], axis=mybir.AxisListType.X),
                              waits=[("v", pg_free)], inc="v")
                    p2 = b.op("vector", lambda E: E.tensor_scalar(SS[:, 1:2], SS[:, 0:1], -1.0, None, A.mult),
                              waits=[("v", p1)], inc="v")
                    pgs_es = b.op("scalar", lambda E: E.activation(ES[:], PGS[:], AF.Exp, bias=SS[:, 1:2]),
                                  waits=[("v", p2)], inc="s")
                    p3 = b.op("vector", lambda E: E.tensor_scalar(IM[:], PGS[:], SS[:, 0:1], None, A.is_ge),
                              waits=[("v", p2)], inc="v")
                    p4 = b.op("vector", lambda E: E.scalar_tensor_tensor(L2[:], IM[:], -1e9, PGS[:], A.mult, A.add),
                              waits=[("v", p3)], inc="v")
                    p5 = b.op("vector", lambda E: E.reduce_max(SS[:, 2:3], L2[:], axis=mybir.AxisListType.X),
                              waits=[("v", p4)], inc="v")
                    p5b = b.op("vector", lambda E: E.tensor_scalar(G2[:], PGS[:], SS[:, 2:3], None, A.is_ge),
                               waits=[("v", p5)], inc="v")
                    vd = b.op("vector", lambda E: E.tensor_tensor(SS[:, 3:4], SS[:, 2:3], SS[:, 0:1], A.subtract),
                              waits=[("v", p5b)], inc="v")
                    se = b.op("scalar", lambda E: E.activation(SS[:, 3:4], SS[:, 3:4], AF.Exp),
                              waits=[("v", vd)], inc="s")
                    p6 = b.op("vector", lambda E: E.tensor_scalar(SS[:, 3:4], SS[:, 3:4], 1.0, None, A.add),
                              waits=[("s", se)], inc="v")
                    p7 = b.op("vector", lambda E: E.reciprocal(SS[:, 3:4], SS[:, 3:4]),
                              waits=[("v", p6)], inc="v")
                    sexp = b.val("s")
                    p8 = b.op("vector", lambda E: E.tensor_tensor(EN[:], ES[:], G2[:], A.mult),
                              waits=[("s", sexp), ("v", p7)], inc="v")
                    b.op("vector", lambda E, tt=tt: E.tensor_scalar(g["comb"][:, tt * 8:(tt + 1) * 8], EN[:], SS[:, 3:4], None, A.mult),
                         waits=[("v", p8)], inc="v")
                if tt % 16 == 15:
                    b.flush()
            b.drain()
            b.flush()

    def phase_proj0(self):
        b, d, g, nc = self.b, self.dram, self.g, self.nc
        with ExitStack() as st:
            sbn = lambda n, sh, ty: st.enter_context(nc.sbuf_tensor(f"pj0_{n}", sh, ty))
            psn = lambda n, sh, ty: st.enter_context(nc.psum_tensor(f"pj0_{n}", sh, ty))
            WQT = sbn("WQT", [128, 8 * D], BF16)
            WKT = sbn("WKT", [128, 8 * D], BF16)
            WVT = sbn("WVT", [128, 8 * D], BF16)
            XC = [sbn("XC0", [128, 8 * 512], BF16), sbn("XC1", [128, 8 * 512], BF16)]
            QS = [sbn("QS0", [128, 512], BF16), sbn("QS1", [128, 512], BF16)]
            KS = [sbn("KS0", [128, 512], BF16), sbn("KS1", [128, 512], BF16)]
            VS = [sbn("VS0", [128, D], BF16), sbn("VS1", [128, D], BF16)]
            pQ = [psn("pQ0", [128, 512], F32), psn("pQ1", [128, 512], F32)]
            pV = [psn("pV0", [128, 512], F32), psn("pV1", [128, 512], F32)]
            for k in range(8):
                b.loadT(WQT[:, k * D:(k + 1) * D], d["wqb"][0:D, k * 128:(k + 1) * 128])
                b.loadT(WKT[:, k * D:(k + 1) * D], d["wkb"][0:D, k * 128:(k + 1) * 128])
                b.loadT(WVT[:, k * D:(k + 1) * D], d["wvb"][0:D, k * 128:(k + 1) * 128])
            wld = [(sname, b.val(sname)) for sname in b.DMA_POOL["sync"][:6]]
            xv = d["xnT"].rearrange("(a p) t -> p a t", p=128)
            xc_free = [0, 0]
            qs_free = [None, None]
            ks_free = [None, None]
            vs_free = [None, None]
            pq_free = [0, 0]
            pv_free = [0, 0]
            for tt4 in range(16):
                ii = tt4 % 2
                t0 = tt4 * 512
                ld = b.load("gpsimd", XC[ii][:].rearrange("p (a c) -> p a c", a=8),
                            xv[:, :, t0:t0 + 512], waits=[("t", xc_free[ii])])
                for (WT, QSb, qf, dstq, bias) in ((WQT, QS, qs_free, d["qt0"], g["bq8"]),
                                                  (WKT, KS, ks_free, d["kt0"], g["bk8"])):
                    for io in range(8):
                        pb = io % 2
                        for k in range(8):
                            b.op("tensor",
                                 lambda E, ii=ii, io=io, k=k, pb=pb, WT=WT: E.matmul(pQ[pb][:], WT[:, k * D + io * 128:k * D + (io + 1) * 128], XC[ii][:, k * 512:(k + 1) * 512], start=(k == 0), stop=(k == 7)),
                                 waits=[(ld if k == 0 else None),
                                        (wld if tt4 == 0 and io == 0 and k == 0 else None),
                                        ("s", pq_free[pb] if k == 0 else 0)],
                                 inc="t" if k == 7 else None)
                        tq = b.val("t")
                        pq_free[pb] = b.op("scalar",
                                           lambda E, io=io, pb=pb, QSb=QSb, bias=bias: E.activation(QSb[io % 2][:], pQ[pb][:], AF.Identity, bias=bias[:, io:io + 1]),
                                           waits=[("t", tq), qf[io % 2]], inc="s")
                        qf[io % 2] = b.store("sync", dstq[io * 128:(io + 1) * 128, t0:t0 + 512],
                                             QSb[io % 2][:], waits=[("s", pq_free[pb])])
                for ts in range(4):
                    sbuf = ts % 2
                    for io2 in range(2):
                        pb = io2
                        for k in range(8):
                            b.op("tensor",
                                 lambda E, ii=ii, ts=ts, io2=io2, k=k, pb=pb: E.matmul(pV[pb][:], XC[ii][:, k * 512 + ts * 128:k * 512 + (ts + 1) * 128], WVT[:, k * D + io2 * 512:k * D + (io2 + 1) * 512], start=(k == 0), stop=(k == 7)),
                                 waits=[("v", pv_free[pb] if k == 0 else 0)],
                                 inc="t" if k == 7 else None)
                        tv = b.val("t")
                        pv_free[pb] = b.op("vector",
                                           lambda E, sbuf=sbuf, io2=io2, pb=pb: E.tensor_tensor(VS[sbuf][:, io2 * 512:(io2 + 1) * 512], pV[pb][:], g["BV"][:, io2 * 512:(io2 + 1) * 512], A.add),
                                           waits=[("t", tv), (vs_free[sbuf] if io2 == 0 else None)], inc="v")
                    vs_free[sbuf] = b.store("sync", d["v0"][t0 + ts * 128:t0 + (ts + 1) * 128, :],
                                            VS[sbuf][:], waits=[("v", pv_free[1])])
                xc_free[ii] = b.val("t")
                if tt4 % 4 == 3:
                    b.flush()
            b.drain()
            b.flush()

    def phase_attn0(self):
        b, d, g, nc = self.b, self.dram, self.g, self.nc
        with ExitStack() as st:
            sbn = lambda n, sh, ty: st.enter_context(nc.sbuf_tensor(f"at0_{n}", sh, ty))
            psn = lambda n, sh, ty: st.enter_context(nc.psum_tensor(f"at0_{n}", sh, ty))
            QH = [sbn("QH0", [64, T], BF16), sbn("QH1", [64, T], BF16)]
            KH = [sbn("KH0", [64, T], BF16), sbn("KH1", [64, T], BF16)]
            VX = [sbn("VX0", [128, 16 * 65], BF16), sbn("VX1", [128, 16 * 65], BF16)]
            LSQ = sbn("LSQ", [64, 16 * R], BF16)
            LSK = sbn("LSK", [64, 16 * R], BF16)
            QLR = sbn("QLR", [32, T], BF16)
            KLR = sbn("KLR", [32, T], BF16)
            PT = [sbn("PT0", [128, 512], BF16), sbn("PT1", [128, 512], BF16)]
            YT = sbn("YT", [128, 8 * T], BF16)
            RC = sbn("RC", [1, 512], F32)
            RCB = sbn("RCB", [1, 512], BF16)
            RS = sbn("RS", [64, 512], F32)
            WOT = sbn("WOT", [128, 8 * D], BF16)
            XR = [sbn("XR0", [128, 512], F32), sbn("XR1", [128, 512], F32)]
            TM = sbn("TM", [128, 512], F32)
            XP = [sbn("XP0", [128, 512], F32), sbn("XP1", [128, 512], F32)]
            pLR = psn("pLR", [128, 512], F32)
            pS = [psn("pS0", [128, 512], F32), psn("pS1", [128, 512], F32)]
            pY = psn("pY", [128, 512], F32)
            pRS = psn("pRS", [128, 512], F32)
            pO = [psn("pO0", [128, 512], F32), psn("pO1", [128, 512], F32)]

            b.load("sync", LSQ[:].rearrange("d (h r) -> d h r", r=R),
                   d["lsrqb"].rearrange("(h d) r -> d h r", d=64))
            b.load("sync", LSK[:].rearrange("d (h r) -> d h r", r=R),
                   d["lsrkb"].rearrange("(h d) r -> d h r", d=64))
            for k in range(8):
                b.loadT(WOT[:, k * D:(k + 1) * D], d["wob"][0:D, k * 128:(k + 1) * 128])
            wld = [(sname, b.val(sname)) for sname in b.DMA_POOL["sync"][:6]]
            for ii in range(2):
                for kb in range(16):
                    b.op("vector", lambda E, ii=ii, kb=kb: E.memset(VX[ii][:, kb * 65 + 64:kb * 65 + 65], 1.0), inc="v")
            ones_v = b.val("v")

            v0v = d["v0"]
            qh_free = [0, 0]
            kh_free = [0, 0]
            vx_free = [0, 0]
            pt_free = [0, 0]
            ps_free = [0, 0]
            plr_free = 0
            prs_free = 0
            py_free = 0
            po_free = [0, 0]
            yt_wo_free = 0
            xr_free = [0, 0]
            xp_free = [None, None]
            ld_q = [None, None]
            ld_k = [None, None]
            ld_v = [None, None]

            def issue_head_loads(s, h):
                ii = h % 2
                ld_q[ii] = b.load("gpsimd", QH[ii][:], d["qt0"][h * 64:(h + 1) * 64, s * T:(s + 1) * T],
                                  waits=[("t", qh_free[ii])])
                ld_k[ii] = b.load("gpsimd", KH[ii][:], d["kt0"][h * 64:(h + 1) * 64, s * T:(s + 1) * T],
                                  waits=[("t", kh_free[ii])])
                ld_v[ii] = b.load("gpsimd",
                                  VX[ii][:].rearrange("p (k c) -> p k c", c=65)[:, :, 0:64],
                                  v0v[s * T:(s + 1) * T, h * 64:(h + 1) * 64].rearrange("(k p) c -> p k c", p=128),
                                  waits=[("t", vx_free[ii]), ("v", ones_v if True else 0)])

            issue_head_loads(0, 0)
            for s in range(NSEQ):
                for h in range(NH):
                    ii = h % 2
                    if h + 1 < NH:
                        issue_head_loads(s, h + 1)
                    elif s + 1 < NSEQ:
                        issue_head_loads(s + 1, 0)
                    lastcp = 0
                    for (SRC, DST, LS, ldw) in ((QH[ii], QLR, LSQ, ld_q[ii]), (KH[ii], KLR, LSK, ld_k[ii])):
                        for c in range(4):
                            mm = b.op("tensor",
                                      lambda E, SRC=SRC, LS=LS, c=c, h=h: E.matmul(pLR[0:32, :], LS[:, h * R:(h + 1) * R], SRC[:, c * 512:(c + 1) * 512], start=True, stop=True),
                                      waits=[ldw if c == 0 else None,
                                             ("v", plr_free)]
                                            + (wld if s == 0 and h == 0 and c == 0 else []),
                                      inc="t")
                            plr_free = b.op("vector",
                                            lambda E, DST=DST, c=c: E.tensor_copy(DST[:, c * 512:(c + 1) * 512], pLR[0:32, :]),
                                            waits=[("t", mm)], inc="v")
                    lastcp = plr_free
                    qh_free[ii] = b.val("t")
                    kh_free[ii] = b.val("t")
                    for qb in range(4):
                        nkb = 4 * qb + 4
                        last_mmy = 0
                        for kb in range(nkb):
                            sb_ = kb % 2
                            mmS = b.op("tensor",
                                       lambda E, sb_=sb_, kb=kb, qb=qb: E.matmul(pS[sb_][:], KLR[:, kb * 128:(kb + 1) * 128], QLR[:, qb * 512:(qb + 1) * 512], start=True, stop=True),
                                       waits=[("v", lastcp if kb == 0 and qb == 0 else 0),
                                              ("s", ps_free[sb_])],
                                       inc="t")
                            ex = b.op("scalar",
                                      lambda E, sb_=sb_: E.activation(PT[sb_][:], pS[sb_][:], AF.Exp, scale=ISCALE),
                                      waits=[("t", max(mmS, pt_free[sb_]))], inc="s")
                            ps_free[sb_] = ex
                            gatev = ("s", ex)
                            j = kb - 4 * qb
                            if j >= 0:
                                vm = 0
                                if j > 0:
                                    b.op("vector", lambda E, sb_=sb_, j=j: E.memset(PT[sb_][:, 0:j * 128], 0.0),
                                         waits=[("s", ex)], inc="v")
                                vm = b.op("vector",
                                          lambda E, sb_=sb_, j=j: E.tensor_tensor(PT[sb_][:, j * 128:(j + 1) * 128], PT[sb_][:, j * 128:(j + 1) * 128], g["triu"][:], A.mult),
                                          waits=[("s", ex)], inc="v")
                                gatev = ("v", vm)
                            last_mmy = b.op("tensor",
                                            lambda E, ii=ii, sb_=sb_, kb=kb, nkb=nkb: E.matmul(pY[0:65, :], VX[ii][:, kb * 65:(kb + 1) * 65], PT[sb_][:], start=(kb == 0), stop=(kb == nkb - 1)),
                                            waits=[gatev, ("v", py_free if kb == 0 else 0),
                                                   (ld_v[ii] if kb == 0 and qb == 0 else None)],
                                            inc="t")
                            pt_free[sb_] = last_mmy
                        vx_free[ii] = last_mmy
                        vr = b.op("vector", lambda E: E.reciprocal(RC[:], pY[64:65, :]),
                                  waits=[("t", last_mmy)], inc="v")
                        vrb = b.op("vector", lambda E: E.tensor_copy(RCB[:], RC[:]),
                                   waits=[("v", vr)], inc="v")
                        mmrs = b.op("tensor", lambda E: E.matmul(pRS[0:64, :], g["ones64"][:], RCB[:], start=True, stop=True),
                                    waits=[("v", vrb), ("v", prs_free)], inc="t")
                        vrs = b.op("vector", lambda E: E.tensor_copy(RS[:], pRS[0:64, :]),
                                   waits=[("t", mmrs)], inc="v")
                        prs_free = vrs
                        py_free = b.op("vector",
                                       lambda E, h=h, qb=qb: E.tensor_tensor(YT[(h % 2) * 64:(h % 2) * 64 + 64, (h // 2) * T + qb * 512:(h // 2) * T + (qb + 1) * 512], pY[0:64, :], RS[:], A.mult),
                                       waits=[("t", yt_wo_free if s > 0 and h == 0 and qb == 0 else 0),
                                              ("v", vrs)],
                                       inc="v")
                    b.flush()
                yt_ready = b.val("v")
                idx = 0
                for ts in range(16):
                    for io2 in range(2):
                        ib = idx % 2
                        ldx = b.load("gpsimd", XR[ib][:],
                                     d["x"][s * T + ts * 128:s * T + (ts + 1) * 128, io2 * 512:(io2 + 1) * 512],
                                     waits=[("v", xr_free[ib])])
                        for jt in range(8):
                            b.op("tensor",
                                 lambda E, ts=ts, io2=io2, jt=jt, ib=ib: E.matmul(pO[ib][:], YT[:, jt * T + ts * 128:jt * T + (ts + 1) * 128], WOT[:, jt * D + io2 * 512:jt * D + (io2 + 1) * 512], start=(jt == 0), stop=(jt == 7)),
                                 waits=[("v", yt_ready if idx == 0 and jt == 0 else 0),
                                        ("v", po_free[ib] if jt == 0 else 0)],
                                 inc="t" if jt == 7 else None)
                        tmo = b.val("t")
                        if ts == 15 and io2 == 1:
                            yt_wo_free = tmo
                        vtm = b.op("vector",
                                   lambda E, io2=io2, ib=ib: E.tensor_tensor(TM[:], pO[ib][:], g["BO"][:, io2 * 512:(io2 + 1) * 512], A.add),
                                   waits=[("t", tmo), ldx], inc="v")
                        vo = b.op("vector",
                                  lambda E, ib=ib: E.tensor_tensor(XP[ib][:], TM[:], XR[ib][:], A.add),
                                  waits=[xp_free[ib], ("v", vtm)], inc="v")
                        po_free[ib] = vo
                        xr_free[ib] = vo
                        xp_free[ib] = b.store("sync",
                                              d["xpart"][s * T + ts * 128:s * T + (ts + 1) * 128, io2 * 512:(io2 + 1) * 512],
                                              XP[ib][:], waits=[("v", vo)])
                        idx += 1
                b.flush()
            b.drain()
            b.flush()

    def phase_moe(self):
        b, d, g, nc = self.b, self.dram, self.g, self.nc
        with ExitStack() as st:
            sbn = lambda n, sh, ty: st.enter_context(nc.sbuf_tensor(f"moe_{n}", sh, ty))
            psn = lambda n, sh, ty: st.enter_context(nc.psum_tensor(f"moe_{n}", sh, ty))
            XG = sbn("XG", [128, 8 * 512], BF16)
            W13 = [sbn("W13_0", [128, 8 * 2 * HID], BF16), sbn("W13_1", [128, 8 * 2 * HID], BF16)]
            W2S = sbn("W2S", [128, 12 * D], BF16)
            GF = sbn("GF", [128, 12 * 512], BF16)
            OA = sbn("OA", [128, 4 * D], F32)
            SIG = sbn("SIG", [128, 512], F32)
            T1 = sbn("T1", [128, 512], F32)
            XPL = [sbn("XPL0", [128, D], F32), sbn("XPL1", [128, D], F32)]
            XO = [sbn("XO0", [128, D], F32), sbn("XO1", [128, D], F32)]
            pH = [psn("pH0", [128, 512], F32), psn("pH1", [128, 512], F32)]
            pGm = [psn("pG0", [128, 512], F32), psn("pG1", [128, 512], F32)]
            pY = [psn("pY0", [128, 512], F32), psn("pY1", [128, 512], F32)]

            w1v = d["w1b"].rearrange("(e k p) h -> e p k h", e=NE, p=128)
            w3v = d["w3b"].rearrange("(e k p) h -> e p k h", e=NE, p=128)
            w2v = d["w2b"].rearrange("(e j p) c -> e p j c", e=NE, p=128)
            xnv = d["xnT"].rearrange("(a p) t -> p a t", p=128)

            w13_free = [0, 0]
            w2_free = 0
            xg_free = 0
            ld13 = [None, None]
            ld2 = None
            hg_free = [0, 0]
            sig_free = 0
            xpl_free = [0, 0]
            gf_free = 0
            py_free = [0, 0]
            py_free_s = [0, 0]
            oa_free = 0

            def load_w13(e, buf):
                t1_ = b.load("sync", W13[buf][:].rearrange("p (k two h) -> p k two h", two=2, h=HID)[:, :, 0, :],
                             w1v[e], waits=[("t", w13_free[buf])])
                t2_ = b.load("sync", W13[buf][:].rearrange("p (k two h) -> p k two h", two=2, h=HID)[:, :, 1, :],
                             w3v[e])
                ld13[buf] = (t1_, t2_)

            load_w13(0, 0)
            for tg in range(16):
                ldx = b.load("gpsimd", XG[:].rearrange("p (k c) -> p k c", k=8),
                             xnv[:, :, tg * 512:(tg + 1) * 512], waits=[("t", xg_free)])
                for e in range(NE):
                    wb = e % 2
                    if e + 1 < NE:
                        load_w13(e + 1, (e + 1) % 2)
                    elif tg + 1 < 16:
                        load_w13(0, 0)
                    ld2 = b.load("sync", W2S[:].rearrange("p (j c) -> p j c", c=D), w2v[e],
                                 waits=[("t", w2_free)])
                    for m in range(12):
                        hb = m % 2
                        for k in range(8):
                            b.op("tensor",
                                 lambda E, m=m, k=k, hb=hb, wb=wb: E.matmul(pH[hb][:], W13[wb][:, k * 2 * HID + m * 128:k * 2 * HID + (m + 1) * 128], XG[:, k * 512:(k + 1) * 512], start=(k == 0), stop=(k == 7)),
                                 waits=[ld13[wb][0] if m == 0 and k == 0 else None,
                                        ld13[wb][1] if m == 0 and k == 0 else None,
                                        ldx if m == 0 and k == 0 else None,
                                        ("v", hg_free[hb] if k == 0 else 0)],
                                 inc="t" if k == 7 else None)
                        for k in range(8):
                            b.op("tensor",
                                 lambda E, m=m, k=k, hb=hb, wb=wb: E.matmul(pGm[hb][:], W13[wb][:, k * 2 * HID + HID + m * 128:k * 2 * HID + HID + (m + 1) * 128], XG[:, k * 512:(k + 1) * 512], start=(k == 0), stop=(k == 7)),
                                 inc="t" if k == 7 else None)
                        tmh = b.val("t")
                        if m == 11:
                            xg_free = tmh
                            w13_free[wb] = tmh
                        sgs = b.op("scalar", lambda E, hb=hb: E.activation(SIG[:], pH[hb][:], AF.Sigmoid),
                                   waits=[("t", tmh), ("v", sig_free)], inc="s")
                        sig_free = b.op("vector", lambda E, hb=hb: E.tensor_tensor(T1[:], pH[hb][:], SIG[:], A.mult),
                                        waits=[("s", sgs), ("t", gf_free if tg + e > 0 and m < 2 else 0)], inc="v")
                        hg_free[hb] = b.op("vector",
                                           lambda E, m=m, hb=hb: E.tensor_tensor(GF[:, m * 512:(m + 1) * 512], pGm[hb][:], T1[:], A.mult),
                                           waits=[("v", sig_free)], inc="v")
                    gf_v = b.val("v")
                    oa_sc_done = b.val("s")
                    for ts in range(4):
                        for io2 in range(2):
                            yb = (ts * 2 + io2) % 2
                            for j in range(12):
                                b.op("tensor",
                                     lambda E, ts=ts, io2=io2, j=j, yb=yb: E.matmul(pY[yb][:], GF[:, j * 512 + ts * 128:j * 512 + (ts + 1) * 128], W2S[:, j * D + io2 * 512:j * D + (io2 + 1) * 512], start=(j == 0), stop=(j == 11)),
                                     waits=[("v", gf_v if ts == 0 and io2 == 0 and j == 0 else 0),
                                            ld2 if ts == 0 and io2 == 0 and j == 0 else None,
                                            ("s", py_free_s[yb] if j == 0 else 0),
                                            ("v", py_free[yb] if j == 0 else 0)],
                                     inc="t" if j == 11 else None)
                            tw2 = b.val("t")
                            if ts == 3 and io2 == 1:
                                gf_free = tw2
                                w2_free = tw2
                            if e == 0:
                                py_free_s[yb] = b.op("scalar",
                                                     lambda E, tg=tg, ts=ts, io2=io2, yb=yb, e=e: E.activation(OA[:, ts * D + io2 * 512:ts * D + (io2 + 1) * 512], pY[yb][:], AF.Identity, scale=g["comb"][:, (tg * 4 + ts) * 8 + e:(tg * 4 + ts) * 8 + e + 1]),
                                                     waits=[("t", tw2), ("v", oa_free if ts == 0 and io2 == 0 else 0)], inc="s")
                            else:
                                py_free[yb] = b.op("vector",
                                                   lambda E, tg=tg, ts=ts, io2=io2, yb=yb, e=e: E.scalar_tensor_tensor(OA[:, ts * D + io2 * 512:ts * D + (io2 + 1) * 512], pY[yb][:], g["comb"][:, (tg * 4 + ts) * 8 + e:(tg * 4 + ts) * 8 + e + 1], OA[:, ts * D + io2 * 512:ts * D + (io2 + 1) * 512], A.mult, A.add),
                                                   waits=[("t", tw2), ("s", oa_sc_done if e == 1 else 0)], inc="v")
                moe_done_v = b.val("v")
                moe_done_s = b.val("s")
                xo_st = [None, None]
                for ts in range(4):
                    ib = ts % 2
                    ldp = b.load("gpsimd", XPL[ib][:], d["xpart"][tg * 512 + ts * 128:tg * 512 + (ts + 1) * 128, :],
                                 waits=[("v", xpl_free[ib])])
                    vxo = b.op("vector",
                               lambda E, ts=ts, ib=ib: E.tensor_tensor(XO[ib][:], OA[:, ts * D:(ts + 1) * D], XPL[ib][:], A.add),
                               waits=[ldp, ("v", moe_done_v if ts == 0 else 0),
                                      ("s", moe_done_s if ts == 0 else 0),
                                      xo_st[ib]], inc="v")
                    xpl_free[ib] = vxo
                    xo_st[ib] = b.store("sync", d["x1"][tg * 512 + ts * 128:tg * 512 + (ts + 1) * 128, :],
                                        XO[ib][:], waits=[("v", vxo)])
                oa_free = b.val("v")
                b.flush()
            b.drain()
            b.flush()

    def phase_proj1(self):
        b, d, g, nc = self.b, self.dram, self.g, self.nc
        with ExitStack() as st:
            sbn = lambda n, sh, ty: st.enter_context(nc.sbuf_tensor(f"pj1_{n}", sh, ty))
            psn = lambda n, sh, ty: st.enter_context(nc.psum_tensor(f"pj1_{n}", sh, ty))
            WQT = sbn("WQT", [128, 8 * D], BF16)
            WKT = sbn("WKT", [128, 8 * D], BF16)
            WVT = sbn("WVT", [128, 8 * D], BF16)
            XC = [sbn("XC0", [128, 8 * 512], BF16), sbn("XC1", [128, 8 * 512], BF16)]
            AS = sbn("AS", [128, 512], F32)
            BS = sbn("BS", [128, 512], F32)
            CS = sbn("CS", [128, 512], F32)
            QS = [sbn("QS0", [128, 512], BF16), sbn("QS1", [128, 512], BF16)]
            KS = [sbn("KS0", [128, D], BF16), sbn("KS1", [128, D], BF16)]
            VS = [sbn("VS0", [128, D], BF16), sbn("VS1", [128, D], BF16)]
            pQ = [psn("pQ0", [128, 512], F32), psn("pQ1", [128, 512], F32)]
            pV = [psn("pV0", [128, 512], F32), psn("pV1", [128, 512], F32)]
            for k in range(8):
                b.loadT(WQT[:, k * D:(k + 1) * D], d["wq1b"][0:D, k * 128:(k + 1) * 128])
                b.loadT(WKT[:, k * D:(k + 1) * D], d["wk1b"][0:D, k * 128:(k + 1) * 128])
                b.loadT(WVT[:, k * D:(k + 1) * D], d["wv1b"][0:D, k * 128:(k + 1) * 128])
            wld = [(sname, b.val(sname)) for sname in b.DMA_POOL["sync"][:6]]
            xv = d["xnT1"].rearrange("(a p) t -> p a t", p=128)
            xc_free = [0, 0]
            qs_free = [None, None]
            ks_free = [None, None]
            vs_free = [None, None]
            pq_free = [0, 0]
            pq_free_s = [0, 0]
            pv_free = [0, 0]

            elu_prev = [0]

            def elu_evac(psrc, dst_tile, tq, extra_waits=()):
                sE = b.op("scalar", lambda E, psrc=psrc: E.activation(CS[:], psrc, AF.Exp),
                          waits=[("t", tq), ("v", elu_prev[0])] + list(extra_waits), inc="s")
                sa = b.op("scalar", lambda E, psrc=psrc: E.activation(AS[:], psrc, AF.Relu), inc="s")
                vm = b.op("vector", lambda E: E.tensor_scalar(BS[:], CS[:], 1.0, None, A.min),
                          waits=[("s", sE)], inc="v")
                vz = b.op("vector", lambda E, dst_tile=dst_tile: E.tensor_tensor(dst_tile, AS[:], BS[:], A.add),
                          waits=[("s", sa), ("v", vm)], inc="v")
                elu_prev[0] = vz
                return vz, sa

            for tt4 in range(16):
                ii = tt4 % 2
                t0 = tt4 * 512
                ld = b.load("gpsimd", XC[ii][:].rearrange("p (a c) -> p a c", a=8),
                            xv[:, :, t0:t0 + 512], waits=[("t", xc_free[ii])])
                for io in range(8):
                    pb = io % 2
                    for k in range(8):
                        b.op("tensor",
                             lambda E, ii=ii, io=io, k=k, pb=pb: E.matmul(pQ[pb][:], WQT[:, k * D + io * 128:k * D + (io + 1) * 128], XC[ii][:, k * 512:(k + 1) * 512], start=(k == 0), stop=(k == 7)),
                             waits=[ld if k == 0 else None,
                                    ("v", pq_free[pb] if k == 0 else 0),
                                    ("s", pq_free_s[pb] if k == 0 else 0)]
                                   + (wld if tt4 == 0 and io == 0 and k == 0 else []),
                             inc="t" if k == 7 else None)
                    tq = b.val("t")
                    vz, sfree = elu_evac(pQ[pb][:], QS[io % 2][:], tq,
                                         extra_waits=[qs_free[io % 2]])
                    pq_free_s[pb] = sfree
                    qs_free[io % 2] = b.store("sync", d["qt1"][io * 128:(io + 1) * 128, t0:t0 + 512],
                                              QS[io % 2][:], waits=[("v", vz)])
                for ts in range(4):
                    sbuf = ts % 2
                    kvz = 0
                    for io2 in range(2):
                        pb = io2
                        for k in range(8):
                            b.op("tensor",
                                 lambda E, ii=ii, ts=ts, io2=io2, k=k, pb=pb: E.matmul(pQ[pb][:], XC[ii][:, k * 512 + ts * 128:k * 512 + (ts + 1) * 128], WKT[:, k * D + io2 * 512:k * D + (io2 + 1) * 512], start=(k == 0), stop=(k == 7)),
                                 waits=[("v", pq_free[pb] if k == 0 else 0),
                                        ("s", pq_free_s[pb] if k == 0 else 0)],
                                 inc="t" if k == 7 else None)
                        tk = b.val("t")
                        extra = [ks_free[sbuf]] if io2 == 0 else []
                        kvz, sfree = elu_evac(pQ[pb][:], KS[sbuf][:, io2 * 512:(io2 + 1) * 512], tk,
                                              extra_waits=extra)
                        pq_free_s[pb] = sfree
                    ks_free[sbuf] = b.store("sync", d["k1tok"][t0 + ts * 128:t0 + (ts + 1) * 128, :],
                                            KS[sbuf][:], waits=[("v", kvz)])
                    for io2 in range(2):
                        pb = io2
                        for k in range(8):
                            b.op("tensor",
                                 lambda E, ii=ii, ts=ts, io2=io2, k=k, pb=pb: E.matmul(pV[pb][:], XC[ii][:, k * 512 + ts * 128:k * 512 + (ts + 1) * 128], WVT[:, k * D + io2 * 512:k * D + (io2 + 1) * 512], start=(k == 0), stop=(k == 7)),
                                 waits=[("v", pv_free[pb] if k == 0 else 0)],
                                 inc="t" if k == 7 else None)
                        tv = b.val("t")
                        pv_free[pb] = b.op("vector",
                                           lambda E, sbuf=sbuf, io2=io2, pb=pb: E.tensor_copy(VS[sbuf][:, io2 * 512:(io2 + 1) * 512], pV[pb][:]),
                                           waits=[("t", tv), (vs_free[sbuf] if io2 == 0 else None)], inc="v")
                    vs_free[sbuf] = b.store("sync", d["v1tok"][t0 + ts * 128:t0 + (ts + 1) * 128, :],
                                            VS[sbuf][:], waits=[("v", pv_free[1])])
                xc_free[ii] = b.val("t")
                if tt4 % 4 == 3:
                    b.flush()
            b.drain()
            b.flush()

    def phase_attn1(self):
        b, d, g, nc = self.b, self.dram, self.g, self.nc
        with ExitStack() as st:
            sbn = lambda n, sh, ty: st.enter_context(nc.sbuf_tensor(f"at1_{n}", sh, ty))
            psn = lambda n, sh, ty: st.enter_context(nc.psum_tensor(f"at1_{n}", sh, ty))
            KX = [sbn("KX0", [128, 16 * 65], BF16), sbn("KX1", [128, 16 * 65], BF16)]
            VXX = [sbn("VXX0", [128, 16 * 65], BF16), sbn("VXX1", [128, 16 * 65], BF16)]
            QH = [sbn("QH0", [64, T], BF16), sbn("QH1", [64, T], BF16)]
            KV = sbn("KV", [64, 65], BF16)
            DN = sbn("DN", [1, 512], F32)
            RCB = sbn("RCB", [1, 512], BF16)
            RS = sbn("RS", [64, 512], F32)
            MT = sbn("MT", [128, 8 * T], BF16)
            XQ = [sbn("XQ0", [128, 514], BF16), sbn("XQ1", [128, 514], BF16)]
            LL = [sbn("LL0", [128, 512], F32), sbn("LL1", [128, 512], F32)]
            WPT = sbn("WPT", [128, 8 * D], BF16)
            XR = [sbn("XR0", [128, 512], F32), sbn("XR1", [128, 512], F32)]
            TM = sbn("TM", [128, 512], F32)
            XP = [sbn("XP0", [128, 512], F32), sbn("XP1", [128, 512], F32)]
            pKV = psn("pKV", [128, 512], F32)
            pA = psn("pA", [128, 512], F32)
            pRS = psn("pRS", [128, 512], F32)
            pO = [psn("pO0", [128, 512], F32), psn("pO1", [128, 512], F32)]

            for k in range(8):
                b.loadT(WPT[:, k * D:(k + 1) * D], d["wpb"][0:D, k * 128:(k + 1) * 128])
            wld = [(sname, b.val(sname)) for sname in b.DMA_POOL["sync"][:6]]
            for ii in range(2):
                for kb in range(16):
                    b.op("vector", lambda E, ii=ii, kb=kb: E.memset(KX[ii][:, kb * 65 + 64:kb * 65 + 65], 1.0), inc="v")
                    b.op("vector", lambda E, ii=ii, kb=kb: E.memset(VXX[ii][:, kb * 65 + 64:kb * 65 + 65], 1.0), inc="v")
            ones_v = b.val("v")

            kx_free = [0, 0]
            vx_free = [0, 0]
            qh_free = [0, 0]
            pkv_free = 0
            pa_free = 0
            prs_free = 0
            po_free = [0, 0]
            mt_wp_free = 0
            xr_free = [0, 0]
            xp_free = [None, None]
            ld_k = [None, None]
            ld_v = [None, None]
            ld_q = [None, None]

            def issue_loads(s, h):
                ii = h % 2
                ld_k[ii] = b.load("gpsimd",
                                  KX[ii][:].rearrange("p (k c) -> p k c", c=65)[:, :, 0:64],
                                  d["k1tok"][s * T:(s + 1) * T, h * 64:(h + 1) * 64].rearrange("(k p) c -> p k c", p=128),
                                  waits=[("t", kx_free[ii]), ("v", ones_v)])
                ld_v[ii] = b.load("gpsimd",
                                  VXX[ii][:].rearrange("p (k c) -> p k c", c=65)[:, :, 0:64],
                                  d["v1tok"][s * T:(s + 1) * T, h * 64:(h + 1) * 64].rearrange("(k p) c -> p k c", p=128),
                                  waits=[("t", vx_free[ii])])
                ld_q[ii] = b.load("gpsimd", QH[ii][:], d["qt1"][h * 64:(h + 1) * 64, s * T:(s + 1) * T],
                                  waits=[("t", qh_free[ii])])

            issue_loads(0, 0)
            for s in range(NSEQ):
                for h in range(NH):
                    ii = h % 2
                    if h + 1 < NH:
                        issue_loads(s, h + 1)
                    elif s + 1 < NSEQ:
                        issue_loads(s + 1, 0)
                    for kb in range(16):
                        b.op("tensor",
                             lambda E, ii=ii, kb=kb: E.matmul(pKV[0:65, :65], KX[ii][:, kb * 65:(kb + 1) * 65], VXX[ii][:, kb * 65:(kb + 1) * 65], start=(kb == 0), stop=(kb == 15)),
                             waits=[ld_k[ii] if kb == 0 else None,
                                    ld_v[ii] if kb == 0 else None,
                                    ("v", pkv_free if kb == 0 else 0)]
                                   + (wld if s == 0 and h == 0 and kb == 0 else []),
                             inc="t" if kb == 15 else None)
                    tkv = b.val("t")
                    kx_free[ii] = tkv
                    vx_free[ii] = tkv
                    pkv_free = b.op("vector", lambda E: E.tensor_copy(KV[:], pKV[0:64, :65]),
                                    waits=[("t", tkv)], inc="v")
                    kvv = pkv_free
                    for tc in range(4):
                        mma = b.op("tensor",
                                   lambda E, ii=ii, tc=tc: E.matmul(pA[0:65, :], KV[:], QH[ii][:, tc * 512:(tc + 1) * 512], start=True, stop=True),
                                   waits=[("v", kvv if tc == 0 else 0),
                                          (ld_q[ii] if tc == 0 else None),
                                          ("v", pa_free if True else 0)],
                                   inc="t")
                        if tc == 3:
                            qh_free[ii] = mma
                        q1 = b.op("vector", lambda E: E.tensor_scalar(DN[:], pA[64:65, :], 1e-6, None, A.max),
                                  waits=[("t", mma)], inc="v")
                        q2 = b.op("vector", lambda E: E.reciprocal(DN[:], DN[:]),
                                  waits=[("v", q1)], inc="v")
                        vrb = b.op("vector", lambda E: E.tensor_copy(RCB[:], DN[:]),
                                   waits=[("v", q2)], inc="v")
                        mmrs = b.op("tensor", lambda E: E.matmul(pRS[0:64, :], g["ones64"][:], RCB[:], start=True, stop=True),
                                    waits=[("v", vrb), ("v", prs_free)], inc="t")
                        vrs = b.op("vector", lambda E: E.tensor_copy(RS[:], pRS[0:64, :]),
                                   waits=[("t", mmrs)], inc="v")
                        prs_free = vrs
                        pa_free = b.op("vector",
                                       lambda E, h=h, tc=tc: E.tensor_tensor(MT[(h % 2) * 64:(h % 2) * 64 + 64, (h // 2) * T + tc * 512:(h // 2) * T + (tc + 1) * 512], pA[0:64, :], RS[:], A.mult),
                                       waits=[("t", mt_wp_free if s > 0 and h == 0 and tc == 0 else 0),
                                              ("v", vrs)],
                                       inc="v")
                    b.flush()
                xq_free = [0, 0]
                ll_free = [0, 0]
                idx = 0
                for ft in range(8):
                    for tc in range(4):
                        ib = idx % 2
                        t0 = s * T + tc * 512
                        if tc == 0:
                            ldq = b.load("gpsimd", XQ[ib][:, 1:514], d["xnT1"][ft * 128:(ft + 1) * 128, t0:t0 + 513],
                                         waits=[("v", xq_free[ib])])
                            b.op("vector", lambda E, ib=ib: E.memset(XQ[ib][:, 0:1], 0.0),
                                 waits=[ldq], inc="v")
                        elif tc == 3:
                            ldq = b.load("gpsimd", XQ[ib][:, 0:513], d["xnT1"][ft * 128:(ft + 1) * 128, t0 - 1:t0 + 512],
                                         waits=[("v", xq_free[ib])])
                            b.op("vector", lambda E, ib=ib: E.memset(XQ[ib][:, 513:514], 0.0),
                                 waits=[ldq], inc="v")
                        else:
                            ldq = b.load("gpsimd", XQ[ib][:, 0:514], d["xnT1"][ft * 128:(ft + 1) * 128, t0 - 1:t0 + 513],
                                         waits=[("v", xq_free[ib])])
                            b.wait("vector", ldq)
                        c1_ = b.op("vector",
                                   lambda E, ft=ft, ib=ib: E.tensor_scalar(LL[ib][:], XQ[ib][:, 0:512], g["cw"][:, ft * 3:ft * 3 + 1], g["cb"][:, ft:ft + 1], A.mult, A.add),
                                   waits=[("v", ll_free[ib])], inc="v")
                        c2_ = b.op("vector",
                                   lambda E, ft=ft, ib=ib: E.scalar_tensor_tensor(LL[ib][:], XQ[ib][:, 1:513], g["cw"][:, ft * 3 + 1:ft * 3 + 2], LL[ib][:], A.mult, A.add),
                                   waits=[("v", c1_)], inc="v")
                        xq_free[ib] = b.op("vector",
                                           lambda E, ft=ft, ib=ib: E.scalar_tensor_tensor(LL[ib][:], XQ[ib][:, 2:514], g["cw"][:, ft * 3 + 2:ft * 3 + 3], LL[ib][:], A.mult, A.add),
                                           waits=[("v", c2_)], inc="v")
                        ll_free[ib] = b.op("vector",
                                           lambda E, ft=ft, tc=tc, ib=ib: E.tensor_tensor(MT[:, ft * T + tc * 512:ft * T + (tc + 1) * 512], MT[:, ft * T + tc * 512:ft * T + (tc + 1) * 512], LL[ib][:], A.add),
                                           waits=[("v", xq_free[ib])], inc="v")
                        idx += 1
                mt_ready = b.val("v")
                b.flush()
                idx = 0
                for ts in range(16):
                    for io2 in range(2):
                        ib = idx % 2
                        ldx = b.load("gpsimd", XR[ib][:],
                                     d["x1"][s * T + ts * 128:s * T + (ts + 1) * 128, io2 * 512:(io2 + 1) * 512],
                                     waits=[("v", xr_free[ib])])
                        for jt in range(8):
                            b.op("tensor",
                                 lambda E, ts=ts, io2=io2, jt=jt, ib=ib: E.matmul(pO[ib][:], MT[:, jt * T + ts * 128:jt * T + (ts + 1) * 128], WPT[:, jt * D + io2 * 512:jt * D + (io2 + 1) * 512], start=(jt == 0), stop=(jt == 7)),
                                 waits=[("v", mt_ready if idx == 0 and jt == 0 else 0),
                                        ("v", po_free[ib] if jt == 0 else 0)],
                                 inc="t" if jt == 7 else None)
                        tmo = b.val("t")
                        if ts == 15 and io2 == 1:
                            mt_wp_free = tmo
                        vtm = b.op("vector",
                                   lambda E, io2=io2, ib=ib: E.tensor_tensor(TM[:], pO[ib][:], g["BP"][:, io2 * 512:(io2 + 1) * 512], A.add),
                                   waits=[("t", tmo), ldx], inc="v")
                        vo = b.op("vector",
                                  lambda E, ib=ib: E.tensor_tensor(XP[ib][:], TM[:], XR[ib][:], A.add),
                                  waits=[xp_free[ib], ("v", vtm)], inc="v")
                        po_free[ib] = vo
                        xr_free[ib] = vo
                        xp_free[ib] = b.store("sync",
                                              d["xpart1"][s * T + ts * 128:s * T + (ts + 1) * 128, io2 * 512:(io2 + 1) * 512],
                                              XP[ib][:], waits=[("v", vo)])
                        idx += 1
                b.flush()
            b.drain()
            b.flush()

    def phase_ffn1a(self):
        b, d, g, nc = self.b, self.dram, self.g, self.nc
        with ExitStack() as st:
            sbn = lambda n, sh, ty: st.enter_context(nc.sbuf_tensor(f"f1a_{n}", sh, ty))
            psn = lambda n, sh, ty: st.enter_context(nc.psum_tensor(f"f1a_{n}", sh, ty))
            W0S = sbn("W0S", [128, 8 * F1], BF16)
            XG = [sbn("XG0", [128, 8 * T], BF16), sbn("XG1", [128, 8 * T], BF16)]
            RST = [sbn("RST0", [128, 512], BF16), sbn("RST1", [128, 512], BF16)]
            pA = [psn("pA0", [128, 512], F32), psn("pA1", [128, 512], F32)]
            w0v = d["w0p"].rearrange("(k p) m -> p k m", p=128)
            for k in range(8):
                b.load("sync", W0S[:].rearrange("p (k m) -> p k m", k=8)[:, k, :], w0v[:, k, :])
            wld = [(sname, b.val(sname)) for sname in b.DMA_POOL["sync"][:6]]
            xnv = d["xnT1"].rearrange("(a p) t -> p a t", p=128)
            xg_free = [0, 0]
            rst_free = [None, None]
            pa_free = [0, 0]
            for tgq in range(4):
                ii = tgq % 2
                ld = b.load("gpsimd", XG[ii][:].rearrange("p (k c) -> p k c", k=8),
                            xnv[:, :, tgq * T:(tgq + 1) * T], waits=[("t", xg_free[ii])])
                for m in range(32):
                    for tc in range(4):
                        pb = (m * 4 + tc) % 2
                        for k in range(8):
                            b.op("tensor",
                                 lambda E, ii=ii, m=m, tc=tc, k=k, pb=pb: E.matmul(pA[pb][:], W0S[:, k * F1 + m * 128:k * F1 + (m + 1) * 128], XG[ii][:, k * T + tc * 512:k * T + (tc + 1) * 512], start=(k == 0), stop=(k == 7)),
                                 waits=[ld if m == 0 and tc == 0 and k == 0 else None,
                                        ("s", pa_free[pb] if k == 0 else 0)]
                                       + (wld if tgq == 0 and m == 0 and tc == 0 and k == 0 else []),
                                 inc="t" if k == 7 else None)
                        ta = b.val("t")
                        sb_ = (m * 4 + tc) % 2
                        pa_free[pb] = b.op("scalar",
                                           lambda E, pb=pb, sb_=sb_: E.activation(RST[sb_][:], pA[pb][:], AF.Relu, scale=g["s0"][:]),
                                           waits=[("t", ta), rst_free[sb_]], inc="s")
                        rst_free[sb_] = b.store("sync",
                                                d["r0T"][m * 128:(m + 1) * 128, tgq * T + tc * 512:tgq * T + (tc + 1) * 512],
                                                RST[sb_][:], waits=[("s", pa_free[pb])])
                xg_free[ii] = b.val("t")
                b.flush()
            b.drain()
            b.flush()

    def phase_ffn1b(self):
        b, d, g, nc = self.b, self.dram, self.g, self.nc
        with ExitStack() as st:
            sbn = lambda n, sh, ty: st.enter_context(nc.sbuf_tensor(f"f1b_{n}", sh, ty))
            psn = lambda n, sh, ty: st.enter_context(nc.psum_tensor(f"f1b_{n}", sh, ty))
            R0G = sbn("R0G", [128, 32 * T], BF16)
            K1S = [sbn("K1S0", [128, F1], BF16), sbn("K1S1", [128, F1], BF16)]
            R1S = [sbn("R1S0", [128, T], BF16), sbn("R1S1", [128, T], BF16)]
            pB = [psn(f"pB{i}", [128, 512], F32) for i in range(8)]
            k1v = d["k1p"].rearrange("(m j p) c -> m p j c", m=32, p=128)
            r0v = d["r0T"].rearrange("(a p) t -> p a t", p=128)
            k1_free = [0, 0]
            r1_free = [None, None]
            pb_free = [0] * 8
            r0_free = 0
            ldk = [None, None]
            ldk[0] = b.load("sync", K1S[0][:].rearrange("p (j c) -> p j c", c=128), k1v[0])
            for tq in range(4):
                ldr = b.load("gpsimd", R0G[:].rearrange("p (a t) -> p a t", a=32),
                             r0v[:, :, tq * T:(tq + 1) * T], waits=[("t", r0_free)])
                for m in range(32):
                    wb = m % 2
                    nm = tq * 32 + m
                    if m + 1 < 32:
                        ldk[(m + 1) % 2] = b.load("sync", K1S[(m + 1) % 2][:].rearrange("p (j c) -> p j c", c=128),
                                                  k1v[m + 1], waits=[("t", k1_free[(m + 1) % 2])])
                    elif tq + 1 < 4:
                        ldk[0] = b.load("sync", K1S[0][:].rearrange("p (j c) -> p j c", c=128),
                                        k1v[0], waits=[("t", k1_free[0])])
                    gen = nm % 2
                    for j in range(32):
                        for tc in range(4):
                            bk = gen * 4 + tc
                            b.op("tensor",
                                 lambda E, m=m, j=j, tc=tc, wb=wb, bk=bk: E.matmul(pB[bk][:], K1S[wb][:, j * 128:(j + 1) * 128], R0G[:, j * T + tc * 512:j * T + (tc + 1) * 512], start=(j == 0), stop=(j == 31)),
                                 waits=[ldk[wb] if j == 0 and tc == 0 else None,
                                        ldr if m == 0 and j == 0 and tc == 0 else None,
                                        ("s", pb_free[bk] if j == 0 else 0)],
                                 inc="t" if j == 31 and tc == 3 else None)
                    tb_ = b.val("t")
                    k1_free[wb] = tb_
                    if m == 31:
                        r0_free = tb_
                    sb_ = m % 2
                    lastg = 0
                    for tc in range(4):
                        bk = gen * 4 + tc
                        pb_free[bk] = b.op("scalar",
                                           lambda E, bk=bk, sb_=sb_, tc=tc: E.activation(R1S[sb_][:, tc * 512:(tc + 1) * 512], pB[bk][:], AF.Relu, scale=g["s1"][:]),
                                           waits=[("t", tb_), (r1_free[sb_] if tc == 0 else None)], inc="s")
                        lastg = pb_free[bk]
                    r1_free[sb_] = b.store("sync",
                                           d["r1T"][m * 128:(m + 1) * 128, tq * T:(tq + 1) * T],
                                           R1S[sb_][:], waits=[("s", lastg)])
                b.flush()
            b.drain()
            b.flush()

    def phase_ffn1c(self):
        b, d, g, nc = self.b, self.dram, self.g, self.nc
        with ExitStack() as st:
            sbn = lambda n, sh, ty: st.enter_context(nc.sbuf_tensor(f"f1c_{n}", sh, ty))
            psn = lambda n, sh, ty: st.enter_context(nc.psum_tensor(f"f1c_{n}", sh, ty))
            K2S = sbn("K2S", [128, 32 * D], BF16)
            R1G = sbn("R1G", [128, 32 * 1024], BF16)
            XPL = [sbn("XPL0", [128, D], F32), sbn("XPL1", [128, D], F32)]
            OS = [sbn("OS0", [128, D], F32), sbn("OS1", [128, D], F32)]
            pC = [psn("pC0", [128, 512], F32), psn("pC1", [128, 512], F32)]
            k2vv = d["k2p"].rearrange("(j p) c -> p j c", p=128)
            wld = b.load("sync", K2S[:].rearrange("p (j c) -> p j c", c=D), k2vv)
            r1v = d["r1T"].rearrange("(a p) t -> p a t", p=128)
            r1_free = 0
            pc_free = [0, 0]
            os_free = [None, None]
            xp_free = [0, 0]
            for tb in range(8):
                ldr = b.load("gpsimd", R1G[:].rearrange("p (a t) -> p a t", a=32),
                             r1v[:, :, tb * 1024:(tb + 1) * 1024], waits=[("t", r1_free)])
                for ts in range(8):
                    ib = ts % 2
                    row0 = tb * 1024 + ts * 128
                    ldp = b.load("gpsimd", XPL[ib][:], d["xpart1"][row0:row0 + 128, :],
                                 waits=[("v", xp_free[ib])])
                    for io2 in range(2):
                        pb = io2
                        for j in range(32):
                            b.op("tensor",
                                 lambda E, ts=ts, io2=io2, j=j, pb=pb: E.matmul(pC[pb][:], R1G[:, j * 1024 + ts * 128:j * 1024 + (ts + 1) * 128], K2S[:, j * D + io2 * 512:j * D + (io2 + 1) * 512], start=(j == 0), stop=(j == 31)),
                                 waits=[ldr if ts == 0 and io2 == 0 and j == 0 else None,
                                        wld if tb == 0 and ts == 0 and io2 == 0 and j == 0 else None,
                                        ("v", pc_free[pb] if j == 0 else 0)],
                                 inc="t" if j == 31 else None)
                        tc_ = b.val("t")
                        if ts == 7 and io2 == 1:
                            r1_free = tc_
                        sc_ = b.op("scalar",
                                   lambda E, ib=ib, io2=io2, pb=pb: E.activation(OS[ib][:, io2 * 512:(io2 + 1) * 512], pC[pb][:], AF.Identity, scale=g["s2"][:]),
                                   waits=[("t", tc_), (os_free[ib] if io2 == 0 else None)], inc="s")
                        pc_free[pb] = b.op("vector",
                                           lambda E, ib=ib, io2=io2: E.tensor_tensor(OS[ib][:, io2 * 512:(io2 + 1) * 512], OS[ib][:, io2 * 512:(io2 + 1) * 512], XPL[ib][:, io2 * 512:(io2 + 1) * 512], A.add),
                                           waits=[("s", sc_), (ldp if io2 == 0 else None)], inc="v")
                    xp_free[ib] = b.val("v")
                    os_free[ib] = b.store("sync", d["y"][row0:row0 + 128, :], OS[ib][:],
                                          waits=[("v", pc_free[1])])
                b.flush()
            b.drain()
            b.flush()


def _prep_inputs(inputs):
    i = inputs
    base = {}
    base["wq"] = np.ascontiguousarray(i["L0_attn_Wq"], np.float32)
    base["wk"] = np.ascontiguousarray(i["L0_attn_Wk"], np.float32)
    base["wv"] = np.ascontiguousarray(i["L0_attn_Wv"], np.float32)
    base["wo"] = np.ascontiguousarray(i["L0_attn_Wo"], np.float32)
    base["wq1"] = np.ascontiguousarray(i["L1_attn_Wq"], np.float32)
    base["wk1"] = np.ascontiguousarray(i["L1_attn_Wk"], np.float32)
    base["wv1"] = np.ascontiguousarray(i["L1_attn_Wv"], np.float32)
    base["wp"] = np.ascontiguousarray(i["L1_attn_Wp"], np.float32)
    base["bq8"] = np.ascontiguousarray(np.asarray(i["L0_attn_bq"], np.float32).reshape(8, 128).T)
    base["bk8"] = np.ascontiguousarray(np.asarray(i["L0_attn_bk"], np.float32).reshape(8, 128).T)
    base["BV"] = np.ascontiguousarray(np.broadcast_to(np.asarray(i["L0_attn_bv"], np.float32), (128, D)))
    base["BO"] = np.ascontiguousarray(np.broadcast_to(np.asarray(i["L0_attn_bo"], np.float32), (128, D)))
    base["BP"] = np.ascontiguousarray(np.broadcast_to(np.asarray(i["L1_attn_bp"], np.float32), (128, D)))
    base["lsrq"] = np.ascontiguousarray(np.asarray(i["L0_attn_Wq_lsr"], np.float32).reshape(D, R))
    base["lsrk"] = np.ascontiguousarray(np.asarray(i["L0_attn_Wk_lsr"], np.float32).reshape(D, R))
    base["wr"] = np.ascontiguousarray(i["L0_ffn_Wr"], np.float32)
    base["w1"] = np.ascontiguousarray(np.asarray(i["L0_ffn_W1"], np.float32).reshape(NE * D, HID))
    base["w3"] = np.ascontiguousarray(np.asarray(i["L0_ffn_W3"], np.float32).reshape(NE * D, HID))
    base["w2"] = np.ascontiguousarray(np.asarray(i["L0_ffn_W2"], np.float32).reshape(NE * HID, D))
    base["cw"] = np.ascontiguousarray(
        np.asarray(i["L1_attn_conv_w"], np.float32).reshape(8, 128, 3).transpose(1, 0, 2).reshape(128, 24))
    base["cb"] = np.ascontiguousarray(np.asarray(i["L1_attn_conv_b"], np.float32).reshape(8, 128).T)
    base["k0"] = np.ascontiguousarray(np.asarray(i["L1_ffn_k0"]).view(np.uint8))
    base["k1"] = np.ascontiguousarray(np.asarray(i["L1_ffn_k1"]).view(np.uint8))
    base["k2"] = np.ascontiguousarray(np.asarray(i["L1_ffn_k2"]).view(np.uint8))
    for nm, key in (("s0c", "L1_ffn_s0"), ("s1c", "L1_ffn_s1"), ("s2c", "L1_ffn_s2")):
        base[nm] = np.ascontiguousarray(np.broadcast_to(np.asarray(key and i[key], np.float32).reshape(1, 1), (128, 1)))
    base["ident"] = np.eye(128, dtype=BF)
    base["triu"] = np.triu(np.ones((128, 128), np.float32), 0).astype(BF)
    x = np.asarray(i["L0_attn_x"], np.float32)
    in_maps = []
    for c in range(8):
        m = dict(base)
        m["x"] = np.ascontiguousarray(x[c * 4:(c + 1) * 4].reshape(NT, D))
        in_maps.append(m)
    return in_maps


def kernel(**inputs):
    in_maps = _prep_inputs(inputs)
    k = Kern()
    nc = k.build()
    res = run_bass_kernel_spmd(nc, in_maps, core_ids=list(range(8)))
    out = np.concatenate([np.asarray(r["y"], np.float32).reshape(4, T, D) for r in res.results], axis=0)
    return out
